# version 15
# baseline (speedup 1.0000x reference)
import numpy as np
import ml_dtypes
import concourse.bass as bass
import concourse.mybir as mybir
from concourse.bass_utils import run_bass_kernel_spmd
from contextlib import ExitStack

F32 = mybir.dt.float32
BF16 = mybir.dt.bfloat16
I32 = mybir.dt.int32
AF = mybir.ActivationFunctionType
ALU = mybir.AluOpType
AX = mybir.AxisListType

N_CORES = 8
D = 4096
S_ALL = 16384
S_CORE = S_ALL // N_CORES
KT = 32
KTP = 16
SC = 4
CH = 512
OQ = 8
ROT = 4
EPS = 1e-5
BF = ml_dtypes.bfloat16


def _perm():
    rho = np.arange(D)
    r = rho // 128
    p = rho % 128
    return 256 * (r // 2) + 64 * (p // 32) + 2 * (p % 32) + (r % 2)


def _host_arrays(data, mask, w1, b1, pk, sc_, zr, b2):
    perm = _perm()
    x2 = np.ascontiguousarray(data.reshape(S_ALL, D))
    xT = np.ascontiguousarray(x2.T)
    w1T = np.ascontiguousarray(w1.T)[:, perm]
    w1b = np.ascontiguousarray(w1T.reshape(KT, 128, KTP, 256).transpose(2, 1, 0, 3))
    pk3 = pk.reshape(D, 64, 32)
    pkb = np.ascontiguousarray(pk3.transpose(1, 2, 0).reshape(KTP, 128, D))
    sTg = sc_.reshape(D, 64).T
    zTg = zr.reshape(D, 64).T
    p_ = np.arange(128)
    szq = np.empty((KTP, 8, 128, 1024), np.float32)
    for ktp in range(KTP):
        rows = 4 * ktp + p_ // 32
        for oq in range(8):
            szq[ktp, oq, :, 0:512] = sTg[rows, oq * 512:(oq + 1) * 512]
            szq[ktp, oq, :, 512:1024] = zTg[rows, oq * 512:(oq + 1) * 512]
    b1p = np.ascontiguousarray(b1[perm].reshape(KT, 128).T)
    b2c = np.ascontiguousarray(b2.reshape(KT, 128).T)
    selh = np.zeros((128, KT, 128), np.float32)
    selth = np.zeros((128, KT, 128), np.float32)
    for mt in range(KT):
        selh[p_, mt, 4 * mt + p_ // 32] = 1.0
        selth[4 * mt + p_ // 32, mt, p_] = 1.0
    return dict(
        xT=xT, w1b=w1b, maskq=np.ascontiguousarray(mask),
        pkb=pkb, szq=szq, b1p=b1p, b2c=b2c,
        selh=np.ascontiguousarray(selh.reshape(128, KT * 128)).astype(BF),
        selth=np.ascontiguousarray(selth.reshape(128, KT * 128)).astype(BF),
        ones32=np.ones((128, 32), np.float32).astype(BF),
        onesrow=np.ones((1, 128), np.float32).astype(BF),
        onesf=np.ones((1, 128), np.float32),
    )


ENG_SEMS = ("PE", "DV", "AC", "PO")


class Emit:
    def __init__(self):
        self.ops = {k: [] for k in ("sync", "gpsimd", "vector", "scalar", "tensor")}
        self.cnt = {}

    def op(self, eng, fn, waits=(), inc=None):
        n = None
        if inc:
            self.cnt[inc] = self.cnt.get(inc, 0) + (1 if inc in ENG_SEMS else 16)
            n = self.cnt[inc]
        self.ops[eng].append((fn, [(s, v) for (s, v) in waits if v > 0], inc))
        return n


def stat_chain(em, ev, nc, xmap, qmap, pe_wait, mrow, rrow, e2row, trow, tag):
    ALU_ = mybir.AluOpType
    AF_ = mybir.ActivationFunctionType
    CHv = CH
    last_ac = 0
    last_dv = 0
    for sc in range(SC):
        pt, base = xmap[sc]
        pt2, base2 = qmap[sc]
        msl = mrow[:, sc * CHv:(sc + 1) * CHv]
        w = [("PE", pe_wait)] if sc == 0 else [("AC", last_ac)]
        em.op("vector", lambda e, pt=pt, base=base, msl=msl: e.tensor_single_scalar(
            msl, pt[base:base + 1, :], 1.0 / D, ALU_.mult), w, "DV")
        em.op("vector", lambda e, pt2=pt2, base2=base2: e.tensor_single_scalar(
            e2row[:], pt2[base2:base2 + 1, :], 1.0 / D, ALU_.mult), [], "DV")
        em.op("vector", lambda e, msl=msl: e.tensor_tensor(
            trow[:], msl, msl, ALU_.mult), [], "DV")
        em.op("vector", lambda e: e.tensor_tensor(
            trow[:], e2row[:], trow[:], ALU_.subtract), [], "DV")
        last_dv = em.op("vector", lambda e: e.tensor_single_scalar(
            trow[:], trow[:], EPS, ALU_.add), [], "DV")
        rsl = rrow[:, sc * CHv:(sc + 1) * CHv]
        em.op("scalar", lambda e: e.activation(
            trow[:], trow[:], AF_.Ln), [("DV", last_dv)], "AC")
        last_ac = em.op("scalar", lambda e, rsl=rsl: e.activation(
            rsl, trow[:], AF_.Exp, scale=-0.5), [], "AC")
    return last_ac


def build(nc):
    d = {}
    for name, shape, dt in (
        ("xT", (D, S_CORE), F32), ("w1b", (KTP, 128, KT, 256), F32),
        ("maskq", (D, D), I32), ("pkb", (KTP, 128, D), I32),
        ("szq", (KTP, 8, 128, 1024), F32),
        ("b1p", (128, KT), F32), ("b2c", (128, KT), F32),
        ("selh", (128, KT * 128), BF16), ("selth", (128, KT * 128), BF16),
        ("ones32", (128, 32), BF16), ("onesrow", (1, 128), BF16),
        ("onesf", (1, 128), F32),
    ):
        d[name] = nc.dram_tensor(name, list(shape), dt, kind="ExternalInput").ap()
    d["outT"] = nc.dram_tensor("outT", [D, S_CORE], F32, kind="ExternalOutput").ap()
    d["x1scr"] = nc.dram_tensor("x1scr", [D, S_CORE], F32, kind="Internal").ap()
    d["w2scr"] = nc.dram_tensor("w2scr", [D, D], BF16, kind="Internal").ap()

    em = Emit()
    ev = {}
    main = ExitStack()
    rowsES = ExitStack()
    sb = lambda n, s, t: main.enter_context(nc.sbuf_tensor(n, s, t))

    h = sb("h", [128, KT * S_CORE], BF16)
    bmall = sb("bmall", [128, KT * 128], BF16)
    zone = sb("zone", [128, 2 * S_CORE], BF16)
    mrep = zone[:, 0:S_CORE]
    rrep = zone[:, S_CORE:2 * S_CORE]
    b1s = sb("b1s", [128, KT], F32)
    b2s = sb("b2s", [128, KT], F32)
    ones32s = sb("ones32s", [128, 32], BF16)
    onesrs = sb("onesrs", [1, 128], BF16)
    onesfs = sb("onesfs", [1, 128], F32)
    resid = sb("resid", [128, CH], F32)
    x1c = sb("x1c", [128, 2 * CH], F32)
    x1bf = sb("x1bf", [128, 2 * CH], BF16)
    xsq2 = sb("xsq2", [128, 2 * CH], BF16)
    bmT01 = sb("bmT01", [128, 128], BF16)
    wscols = sb("wscols", [128, KT], F32)
    rsb = lambda n, s, t: rowsES.enter_context(nc.sbuf_tensor(n, s, t))
    mrow = rsb("mrow", [1, S_CORE], BF16)
    rrow = rsb("rrow", [1, S_CORE], BF16)
    e2row = rsb("e2row", [1, CH], F32)
    trow = rsb("trow", [1, CH], F32)

    def W(sem):
        return lambda v: (sem, v)
    PEw, DVw, ACw = W("PE"), W("DV"), W("AC")

    def dma(eng, dst, src, waits=(), q=None):
        return em.op(eng, lambda e, dst=dst, src=src: e.dma_start(out=dst, in_=src),
                     waits, q)

    def rap(dap, off, dims):
        return bass.AP(dap.tensor, off, dims)

    dma("sync", b1s[:], d["b1p"][:], q="MA")
    dma("sync", b2s[:], d["b2c"][:], q="MA")
    dma("sync", ones32s[:], d["ones32"][:], q="MA")
    dma("sync", onesrs[:], d["onesrow"][:], q="MA")
    dma("sync", onesfs[:], d["onesf"][:], q="MA")
    ev["smalls"] = em.cnt["MA"]

    for kt in range(KT):
        ev[("xload", kt)] = dma(
            "gpsimd", h[:, kt * S_CORE:(kt + 1) * S_CORE],
            rap(d["xT"], kt * 128 * S_CORE, [[S_CORE, 128], [1, S_CORE]]), q="XA")

    with ExitStack() as pha:
        psb = lambda n, s, t: pha.enter_context(nc.sbuf_tensor(n, s, t))
        pps = lambda n, s, t: pha.enter_context(nc.psum_tensor(n, s, t))
        selS = psb("selS", [128, KT * 128], BF16)
        maskb = psb("maskb", [128, 2 * 2048], I32)
        colsum = psb("colsum", [128, 2 * 128], I32)
        colsumf = psb("colsumf", [128, 2 * 128], BF16)
        xsq1 = psb("xsq1", [128, 2 * CH], BF16)
        p_bmT = pps("p_bmT", [128, 128], F32)
        p_bmall = pps("p_bmall", [128, 128], F32)
        p_sxA = pps("p_sxA", [128, CH], F32)
        p_sxB = pps("p_sxB", [128, CH], F32)
        p_sqA = pps("p_sqA", [128, CH], F32)
        p_sqB = pps("p_sqB", [128, CH], F32)
        sxmap = [(p_sxA, 0), (p_sxA, 64), (p_sxB, 0), (p_sxB, 64)]
        sqmap = [(p_sqA, 0), (p_sqA, 64), (p_sqB, 0), (p_sqB, 64)]

        dma("sync", selS[:], d["selh"][:], q="MA")
        ev["sel"] = em.cnt["MA"]

        for mt in range(KT):
            for chm in range(2):
                q = 2 * mt + chm
                ev[("mload", q)] = dma(
                    "sync", maskb[:, (q % 2) * 2048:(q % 2 + 1) * 2048],
                    rap(d["maskq"], mt * 128 * D + chm * 2048,
                        [[D, 128], [1, 2048]]),
                    waits=[DVw(ev.get(("mred", q - 2), 0))], q="MA")
                mb = maskb[:, (q % 2) * 2048:(q % 2 + 1) * 2048].rearrange(
                    "p (a b) -> p a b", b=32)
                cs = colsum[:, (mt % 2) * 128 + chm * 64:
                            (mt % 2) * 128 + chm * 64 + 64]
                def red(e, mb=mb, cs=cs):
                    with nc.allow_low_precision(reason="int32 block sums, exact"):
                        return e.tensor_reduce(cs, mb, AX.X, ALU.add)
                ev[("mred", q)] = em.op(
                    "vector", red,
                    [("MA", ev[("mload", q)]), ACw(ev.get(("mcast", mt - 2), 0))],
                    "DV")
            cs128 = colsum[:, (mt % 2) * 128:(mt % 2 + 1) * 128]
            cf = colsumf[:, (mt % 2) * 128:(mt % 2 + 1) * 128]
            ev[("mcast", mt)] = em.op(
                "scalar", lambda e, cs128=cs128, cf=cf: e.copy(cf, cs128),
                [DVw(ev[("mred", 2 * mt + 1)]), PEw(ev.get(("mmm", mt - 2), 0))],
                "AC")
            rhs = selS[:, mt * 128:(mt + 1) * 128]
            ev[("mmm", mt)] = em.op(
                "tensor",
                lambda e, cf=cf, rhs=rhs, mt=mt: e.matmul(
                    p_bmT[:], cf, rhs, start=(mt == 0), stop=(mt == KT - 1)),
                [ACw(ev[("mcast", mt)]), ("MA", ev["sel"])], "PE")
        ev["isgt"] = em.op(
            "vector",
            lambda e: e.tensor_single_scalar(bmT01[:], p_bmT[:], 0.0, ALU.is_gt),
            [PEw(ev[("mmm", KT - 1)])], "DV")
        ev["selthload"] = dma("scalar", selS[:], d["selth"][:],
                              waits=[PEw(ev[("mmm", KT - 1)])], q="SE")
        for kt in range(KT):
            lhs = selS[:, kt * 128:(kt + 1) * 128]
            ev[("bmallmm", kt)] = em.op(
                "tensor",
                lambda e, lhs=lhs: e.matmul(p_bmall[:], lhs, bmT01[:],
                                            start=True, stop=True),
                [DVw(max(ev["isgt"], ev.get(("bmallev", kt - 1), 0))),
                 ("SE", ev["selthload"])], "PE")
            ev[("bmallev", kt)] = em.op(
                "vector",
                lambda e, kt=kt: e.tensor_copy(
                    bmall[:, kt * 128:(kt + 1) * 128], p_bmall[:]),
                [PEw(ev[("bmallmm", kt)])], "DV")
        ev["bmall_done"] = ev[("bmallev", KT - 1)]

        for kt in range(KT):
            for sc in range(SC):
                hsl = h[:, kt * S_CORE + sc * CH: kt * S_CORE + (sc + 1) * CH]
                slot = (kt * SC + sc) % 2
                xq = xsq1[:, slot * CH:(slot + 1) * CH]
                pidx = kt * SC + sc - 2
                pkey = ("s1q", pidx // SC, pidx % SC) if pidx >= 0 else None
                ev[("sq1", kt, sc)] = em.op(
                    "scalar",
                    lambda e, hsl=hsl, xq=xq: e.activation(xq, hsl, AF.Square),
                    [("XA", ev[("xload", kt)]),
                     PEw(ev.get(pkey, 0) if pkey else 0)], "AC")
                pt, base = sxmap[sc]
                ev[("s1x", kt, sc)] = em.op(
                    "tensor",
                    lambda e, pt=pt, base=base, hsl=hsl, kt=kt: e.matmul(
                        pt[base:base + 32, :], ones32s[:], hsl,
                        start=(kt == 0), stop=(kt == KT - 1),
                        tile_position=(0, base)),
                    [("XA", ev[("xload", kt)]), ("MA", ev["smalls"])], "PE")
                pt, base = sqmap[sc]
                ev[("s1q", kt, sc)] = em.op(
                    "tensor",
                    lambda e, pt=pt, base=base, xq=xq, kt=kt: e.matmul(
                        pt[base:base + 32, :], ones32s[:], xq,
                        start=(kt == 0), stop=(kt == KT - 1),
                        tile_position=(0, base)),
                    [ACw(ev[("sq1", kt, sc)])], "PE")
        ev["s1done"] = em.cnt["PE"]

        ev["rst1"] = stat_chain(em, ev, nc, sxmap, sqmap, ev["s1done"],
                                mrow, rrow, e2row, trow, "c1")
        ev["scopeA_pe"] = ev["s1done"]
        ev["scopeA_dv"] = em.cnt["DV"]
        ev["scopeA_ac"] = em.cnt["AC"]

    pp = [main.enter_context(nc.psum_tensor(f"pp{i}", [128, CH], F32))
          for i in range(ROT)]
    p_s2xA = main.enter_context(nc.psum_tensor("p_s2xA", [128, CH], F32))
    p_s2qA = main.enter_context(nc.psum_tensor("p_s2qA", [128, CH], F32))
    p_wsA = main.enter_context(nc.psum_tensor("p_wsA", [128, CH], F32))
    p_wsB = main.enter_context(nc.psum_tensor("p_wsB", [128, CH], F32))
    s2xmap = [(p_s2xA, 0), (p_s2xA, 32), (p_s2xA, 64), (p_s2xA, 96)]
    s2qmap = [(p_s2qA, 0), (p_s2qA, 32), (p_s2qA, 64), (p_s2qA, 96)]
    wsmap = [(p_wsA, 0), (p_wsA, 32), (p_wsA, 64), (p_wsA, 96),
             (p_wsB, 0), (p_wsB, 32), (p_wsB, 64), (p_wsB, 96)]

    def rep_build(tag, first_waits, mdst=None):
        wl = list(first_waits)
        last = 0
        for idx, (src, dst) in enumerate(((mrow, mdst if mdst is not None else mrep),
                                          (rrow, rrep))):
            for c in range(SC):
                mm = em.op(
                    "tensor",
                    lambda e, src=src, c=c: e.matmul(
                        pp[0][:], onesrs[:], src[:, c * CH:(c + 1) * CH],
                        start=True, stop=True),
                    wl + [DVw(last)], "PE")
                wl = []
                last = em.op(
                    "vector",
                    lambda e, dst=dst, c=c: e.tensor_copy(
                        dst[:, c * CH:(c + 1) * CH], pp[0][:]),
                    [PEw(mm)], "DV")
        return last

    ev["rep1"] = rep_build("rep1", [ACw(ev["rst1"])])

    first = True
    for kt in range(KT):
        for sc in range(SC):
            hsl = h[:, kt * S_CORE + sc * CH: kt * S_CORE + (sc + 1) * CH]
            msl = mrep[:, sc * CH:(sc + 1) * CH]
            rsl = rrep[:, sc * CH:(sc + 1) * CH]
            w = [PEw(ev["s1done"])] if first else []
            first = False
            em.op("vector",
                  lambda e, hsl=hsl, msl=msl: e.tensor_tensor(
                      hsl, hsl, msl, ALU.subtract), w, "DV")
            em.op("vector",
                  lambda e, hsl=hsl, rsl=rsl: e.tensor_tensor(
                      hsl, hsl, rsl, ALU.mult), [], "DV")
    ev["aff1done"] = em.cnt["DV"]

    scopeB = ExitStack()
    bsb = lambda n, s, t: scopeB.enter_context(nc.sbuf_tensor(n, s, t))
    w1s = bsb("w1s", [128, 2 * 2 * 4096], BF16)
    pkbf = bsb("pkbf", [128, 2 * 512], I32)
    lob = bsb("lob", [128, 512], I32)
    w2e = bsb("w2e", [128, 512], BF16)
    w2o_t = bsb("w2o_t", [128, 512], BF16)
    szbd = zone[:].bitcast(F32)

    def emit_w1(ktp, first=False):
        buf = ktp % 2
        for half in range(2):
            if first and half == 0:
                w = [PEw(ev["scopeA_pe"]), DVw(ev["scopeA_dv"]), ACw(ev["scopeA_ac"])]
            else:
                w = [DVw(ev.get(("mtt", ktp - 2, half, 1), 0)),
                     PEw(ev.get(("g_stop", (ktp - 2) * 8 + 7), 0))]
            ev[("w1load", ktp, half)] = dma(
                "gpsimd",
                w1s[:, (buf * 2 + half) * 4096:(buf * 2 + half + 1) * 4096],
                rap(d["w1b"], ktp * 128 * KT * 256 + half * 16 * 256,
                    [[KT * 256, 128], [1, 16 * 256]]),
                waits=w, q="W1")

    def emit_mtt_one(ktp, half, par):
        buf = ktp % 2
        base0 = (buf * 2 + half) * 4096 + par * 128
        srcap = bass.AP(w1s[:].tensor, base0,
                        [[2 * 2 * 4096, 128], [256, 16], [32, 4], [16, 2], [1, 16]])
        in1 = bass.AP(bmall[:].tensor, half * 16 * 128 + 8 * ktp,
                      [[KT * 128, 128], [128, 16], [2, 4], [1, 2], [0, 16]])
        ev[("mtt", ktp, half, par)] = em.op(
            "vector",
            lambda e, srcap=srcap, in1=in1: e.tensor_tensor(
                srcap, srcap, in1, ALU.mult),
            [("W1", ev[("w1load", ktp, half)]), DVw(ev["bmall_done"]),
             PEw(ev.get(("g_stop", (ktp - 2) * 8 + 7), 0))], "DV")

    def emit_mtt(ktp):
        for half in range(2):
            for par in range(2):
                emit_mtt_one(ktp, half, par)

    def emit_deq_chunk(ktp, oq):
        qb = oq % 2
        qidx = ktp * OQ + oq
        ev[("pkload", qidx)] = dma(
            "sync", pkbf[:, qb * 512:(qb + 1) * 512],
            rap(d["pkb"], ktp * 128 * D + oq * 512, [[D, 128], [1, 512]]),
            waits=[DVw(ev.get(("deq", qidx - 2), 0))] if qidx >= 2
            else [PEw(ev["scopeA_pe"]), DVw(ev["scopeA_dv"]), ACw(ev["scopeA_ac"])],
            q="PK")
        srb = szbd[:, qb * 1024:qb * 1024 + 512]
        zrb = szbd[:, qb * 1024 + 512:(qb + 1) * 1024]
        ev[("szload", qidx)] = dma(
            "sync", szbd[:, qb * 1024:(qb + 1) * 1024],
            rap(d["szq"], (ktp * 8 + oq) * 128 * 1024, [[1024, 128], [1, 1024]]),
            waits=[DVw(ev.get(("deq", qidx - 2), ev["aff1done"]))], q="SZ")
        pks = pkbf[:, qb * 512:(qb + 1) * 512]
        em.op("vector",
              lambda e, pks=pks: e.tensor_single_scalar(
                  lob[:], pks, 15, ALU.bitwise_and),
              [("PK", ev[("pkload", qidx)])], "DV")
        em.op("vector",
              lambda e, pks=pks: e.tensor_single_scalar(
                  pks, pks, 4, ALU.logical_shift_right), [], "DV")
        em.op("vector",
              lambda e, pks=pks: e.tensor_single_scalar(
                  pks, pks, 15, ALU.bitwise_and), [], "DV")
        ev[("w2ett", qidx)] = em.op(
            "vector",
            lambda e, srb=srb: e.tensor_tensor(w2e[:], lob[:], srb, ALU.mult),
            [("SZ", ev[("szload", qidx)]),
             ("WW", ev.get(("w2w", qidx - 1, 0), 0)),
             PEw(ev.get(("wsum", qidx - 1), 0))], "DV")
        em.op("vector",
              lambda e, zrb=zrb: e.tensor_tensor(w2e[:], w2e[:], zrb, ALU.add),
              [], "DV")
        em.op("vector",
              lambda e, pks=pks, srb=srb: e.tensor_tensor(
                  w2o_t[:], pks, srb, ALU.mult),
              [("WW", ev.get(("w2w", qidx - 1, 1), 0))], "DV")
        ev[("deq", qidx)] = em.op(
            "vector",
            lambda e, zrb=zrb: e.tensor_tensor(w2o_t[:], w2o_t[:], zrb, ALU.add),
            [], "DV")
        for par2, wsrc in ((0, w2e), (1, w2o_t)):
            ev[("w2w", qidx, par2)] = dma(
                "scalar",
                rap(d["w2scr"], (256 * ktp + 128 * par2) * D + oq * 512,
                    [[D, 128], [1, 512]]),
                wsrc[:], waits=[DVw(ev[("deq", qidx)])], q="WW")
        pt, base = wsmap[oq]
        def _ws(pt=pt, base=base, ktp=ktp, qidx=qidx):
            for par2, wsrc in ((0, w2e), (1, w2o_t)):
                em.op("tensor",
                      lambda e, wsrc=wsrc, par2=par2: e.matmul(
                          pt[base:base + 32, :], ones32s[:], wsrc[:],
                          start=(ktp == 0 and par2 == 0),
                          stop=(ktp == KTP - 1 and par2 == 1),
                          tile_position=(0, base)),
                      [DVw(ev[("deq", qidx)])] if par2 == 0 else [], "PE")
            ev[("wsum", qidx)] = em.cnt["PE"]
        ws_pending.append(_ws)

    def emit_stats2(g):
        r2, sc2 = divmod(g, SC)
        xb = x1bf[:, (g % 2) * CH:(g % 2 + 1) * CH]
        xq = xsq2[:, (g % 2) * CH:(g % 2 + 1) * CH]
        pt, base = s2xmap[sc2]
        em.op("tensor",
              lambda e, pt=pt, base=base, xb=xb, r2=r2: e.matmul(
                  pt[base:base + 32, :], ones32s[:], xb,
                  start=(r2 == 0), stop=(r2 == KT - 1),
                  tile_position=(0, base)),
              [ACw(ev[("cast2", g)])], "PE")
        pt2, base2 = s2qmap[sc2]
        ev[("s2q", g)] = em.op(
            "tensor",
            lambda e, pt2=pt2, base2=base2, xq=xq, r2=r2: e.matmul(
                pt2[base2:base2 + 32, :], ones32s[:], xq,
                start=(r2 == 0), stop=(r2 == KT - 1),
                tile_position=(0, base2)),
            [], "PE")

    from collections import deque
    ws_pending = deque()
    emit_w1(0, first=True)
    emit_mtt(0)
    for ktp in range(KTP):
        buf = ktp % 2
        for par in range(2):
            r = 2 * ktp + par
            for sc in range(SC):
                g = r * SC + sc
                j = par * SC + sc
                emit_deq_chunk(ktp, j)
                ev[("res", g)] = dma(
                    "sync", resid[:],
                    rap(d["xT"], (256 * ktp + par) * S_CORE + sc * CH,
                        [[64 * S_CORE, 4], [2 * S_CORE, 32], [1, CH]]),
                    waits=[DVw(ev.get(("evac", g - 1), 0))], q="RE")
                bank = g % ROT
                w = [DVw(max(ev[("mtt", ktp, 0, par)], ev[("mtt", ktp, 1, par)],
                             ev["aff1done"], ev.get(("evac", g - ROT), 0)))]
                for ktK in range(KT):
                    half, ktl = ktK // 16, ktK % 16
                    lhs = w1s[:, (buf * 2 + half) * 4096 + ktl * 256 + par * 128:
                              (buf * 2 + half) * 4096 + ktl * 256 + par * 128 + 128]
                    rhs = h[:, ktK * S_CORE + sc * CH: ktK * S_CORE + (sc + 1) * CH]
                    em.op("tensor",
                          lambda e, lhs=lhs, rhs=rhs, bank=bank, ktK=ktK: e.matmul(
                              pp[bank][:], lhs, rhs,
                              start=(ktK == 0), stop=(ktK == KT - 1)),
                          w, "PE" if ktK == KT - 1 else None)
                    w = []
                ev[("g_stop", g)] = em.cnt["PE"]
                xs = x1c[:, (g % 2) * CH:(g % 2 + 1) * CH]
                rs = resid[:]
                em.op("vector",
                      lambda e, xs=xs, rs=rs, bank=bank: e.tensor_tensor(
                          xs, pp[bank][:], rs, ALU.add),
                      [PEw(ev[("g_stop", g)]), ("RE", ev[("res", g)]),
                       ACw(ev.get(("cast2", g - 2), 0)),
                       ("XW", ev.get(("x1w", g - 2), 0))], "DV")
                ev[("evac", g)] = em.op(
                    "vector",
                    lambda e, xs=xs, r=r: e.tensor_scalar(
                        xs, xs, b1s[:, r:r + 1], None, ALU.add), [], "DV")
                ev[("x1w", g)] = dma(
                    "scalar",
                    rap(d["x1scr"], 128 * r * S_CORE + sc * CH,
                        [[S_CORE, 128], [1, CH]]),
                    xs, waits=[DVw(ev[("evac", g)])], q="XW")
                xb = x1bf[:, (g % 2) * CH:(g % 2 + 1) * CH]
                xq = xsq2[:, (g % 2) * CH:(g % 2 + 1) * CH]
                em.op("scalar",
                      lambda e, xb=xb, xs=xs: e.activation(xb, xs, AF.Copy),
                      [DVw(ev[("evac", g)]), PEw(ev.get(("s2q", g - 2), 0))], "AC")
                ev[("cast2", g)] = em.op(
                    "scalar",
                    lambda e, xq=xq, xs=xs: e.activation(xq, xs, AF.Square),
                    [], "AC")
                if g > 0:
                    emit_stats2(g - 1)
                while ws_pending:
                    ws_pending.popleft()()
                if j == 0 and ktp + 1 < KTP:
                    emit_w1(ktp + 1)
                if 3 <= j <= 6 and ktp + 1 < KTP:
                    hp = ((0, 0), (1, 0), (0, 1), (1, 1))[j - 3]
                    emit_mtt_one(ktp + 1, hp[0], hp[1])
    emit_stats2(KT * SC - 1)
    while ws_pending:
        ws_pending.popleft()()
    ev["l1_pe_done"] = em.cnt["PE"]
    ev["deq_dv_done"] = em.cnt["DV"]
    ev["ww_all"] = em.cnt["WW"]
    ev["xw_all"] = em.cnt["XW"]
    ev["w2w_done"] = ev[("w2w", KTP * OQ - 1, 1)]
    ev["cast2_done"] = ev[("cast2", KT * SC - 1)]
    ev["x1w_done"] = ev[("x1w", KT * SC - 1)]
    wsrow = lob.bitcast(F32)
    last = 0
    for oq in range(8):
        pt, base = wsmap[oq]
        em.op("vector",
              lambda e, pt=pt, base=base: e.tensor_copy(
                  wsrow[0:1, :], pt[base:base + 1, :]),
              [PEw(ev["l1_pe_done"])] if oq == 0 else [], "DV")
        for otl in range(4):
            ot = oq * 4 + otl
            mm = em.op("tensor",
                       lambda e, otl=otl: e.matmul(
                           pp[0][:, 0:1], wsrow[0:1, otl * 128:(otl + 1) * 128],
                           onesfs[0:1, 0:1], start=True, stop=True),
                       [DVw(em.cnt["DV"]), DVw(last)], "PE")
            last = em.op("vector",
                         lambda e, ot=ot: e.tensor_copy(
                             wscols[:, ot:ot + 1], pp[0][:, 0:1]),
                         [PEw(mm)], "DV")
    ev["wscols"] = last
    scopeB.close()
    ev["rst2"] = stat_chain(em, ev, nc, s2xmap, s2qmap, ev["l1_pe_done"],
                            mrow, rrow, e2row, trow, "c2")
    em.op("vector", lambda e: e.tensor_tensor(mrow[:], mrow[:], rrow[:], ALU.mult),
          [ACw(ev["rst2"])], "DV")
    ev["negrm"] = em.op(
        "vector", lambda e: e.tensor_single_scalar(mrow[:], mrow[:], -1.0, ALU.mult),
        [], "DV")
    ev["rep2"] = rep_build("rep2", [DVw(ev["negrm"])])

    for r in range(KT):
        w = [("XW", min(ev["xw_all"], ev[("x1w", r * SC + SC - 1)] + 16 * 12)),
             PEw(ev["l1_pe_done"])]
        ev[("x1load", r)] = dma(
            "gpsimd", h[:, r * S_CORE:(r + 1) * S_CORE],
            rap(d["x1scr"], 128 * r * S_CORE, [[S_CORE, 128], [1, S_CORE]]),
            waits=w, q="XC")
    ev["xc_all"] = em.cnt["XC"]

    w2ob = w1s

    for otp in range(KTP):
        buf = otp % 2
        w = [("WW", ev["ww_all"]), DVw(ev["deq_dv_done"]),
             PEw(ev.get(("g2_stop", (otp - 2) * 8 + 7), ev["l1_pe_done"]))]
        ev[("w2oload", otp)] = dma(
            "sync",
            w2ob[:, buf * 8192:(buf + 1) * 8192].rearrange("p (r c) -> p r c", c=256),
            rap(d["w2scr"], otp * 256, [[D, 128], [128 * D, 32], [1, 256]]),
            waits=w, q="WO")
        for otl in range(2):
            ot = 2 * otp + otl
            for sc in range(SC):
                g2 = otp * 8 + otl * 4 + sc
                bank = g2 % ROT
                kq = ot // 2
                off = (256 * kq + 64 * (ot % 2)) * S_CORE + sc * CH
                wre = ([DVw(ev[("evac2", g2 - 1)])] if g2 > 0
                       else [("XW", ev["xw_all"])])
                ev[("res2", g2)] = dma(
                    "gpsimd",
                    resid[:],
                    rap(d["x1scr"], off,
                        [[32 * S_CORE, 2], [S_CORE, 32], [128 * S_CORE, 2], [1, CH]]),
                    waits=wre, q="R2")
                w = [("WO", ev[("w2oload", otp)]), ("XC", ev["xc_all"]),
                     DVw(max(ev["wscols"], ev.get(("evac2", g2 - ROT), 0)))]
                for rk in range(KT):
                    lhs = w2ob[:, buf * 8192 + rk * 256 + otl * 128:
                               buf * 8192 + rk * 256 + otl * 128 + 128]
                    rhs = h[:, rk * S_CORE + sc * CH: rk * S_CORE + (sc + 1) * CH]
                    em.op("tensor",
                          lambda e, lhs=lhs, rhs=rhs, bank=bank, rk=rk: e.matmul(
                              pp[bank][:], lhs, rhs,
                              start=(rk == 0), stop=(rk == KT - 1)),
                          w, "PE" if rk == KT - 1 else None)
                    w = []
                ev[("g2_stop", g2)] = em.cnt["PE"]
                xs = x1c[:, (g2 % 2) * CH:(g2 % 2 + 1) * CH]
                rs = resid[:]
                rsl2 = rrep[:, sc * CH:(sc + 1) * CH]
                msl2 = mrep[:, sc * CH:(sc + 1) * CH]
                wev = [PEw(ev[("g2_stop", g2)]), ("R2", ev[("res2", g2)]),
                       ("OW", ev.get(("outw", g2 - 2), 0))]
                if g2 < 2:
                    wev += [ACw(ev["cast2_done"]), ("XW", ev["xw_all"])]
                em.op("vector",
                      lambda e, xs=xs, rsl2=rsl2, bank=bank: e.tensor_tensor(
                          xs, pp[bank][:], rsl2, ALU.mult), wev, "DV")
                em.op("vector",
                      lambda e, xs=xs, msl2=msl2, ot=ot: e.scalar_tensor_tensor(
                          xs, msl2, wscols[:, ot:ot + 1], xs, ALU.mult, ALU.add),
                      [], "DV")
                em.op("vector",
                      lambda e, xs=xs, rs=rs: e.tensor_tensor(
                          xs, xs, rs, ALU.add), [], "DV")
                ev[("evac2", g2)] = em.op(
                    "vector",
                    lambda e, xs=xs, ot=ot: e.tensor_scalar(
                        xs, xs, b2s[:, ot:ot + 1], None, ALU.add), [], "DV")
                ev[("outw", g2)] = dma(
                    "scalar",
                    rap(d["outT"], 128 * ot * S_CORE + sc * CH,
                        [[S_CORE, 128], [1, CH]]),
                    xs, waits=[DVw(ev[("evac2", g2)])], q="OW")

    allw = [(s, em.cnt[s]) for s in em.cnt.keys() if s not in ENG_SEMS]
    em.op("scalar", lambda e: e.nop(), [("OW", em.cnt["OW"])], None)
    em.op("sync", lambda e: e.nop(), allw, None)

    semES = ExitStack()
    sems = {name: semES.enter_context(nc.semaphore("sem_" + name))
            for name in em.cnt.keys()}
    with nc.Block() as block:

        def run(e, lst):
            for fn, waits, inc in lst:
                for s, v in waits:
                    e.wait_ge(sems[s], v)
                ins = fn(e)
                if inc:
                    ins.then_inc(sems[inc], 1 if inc in ENG_SEMS else 16)

        @block.sync
        def _(eng):
            run(eng, em.ops["sync"])

        @block.gpsimd
        def _(eng):
            run(eng, em.ops["gpsimd"])

        @block.vector
        def _(eng):
            run(eng, em.ops["vector"])

        @block.scalar
        def _(eng):
            run(eng, em.ops["scalar"])

        @block.tensor
        def _(eng):
            run(eng, em.ops["tensor"])

    rowsES.close()
    semES.close()
    main.close()
    return nc


def kernel(L0_attn_data, L0_attn_mask, L0_attn_weight, L0_attn_bias,
           L0_ffn_weight_packed, L0_ffn_weight_scales, L0_ffn_weight_zeros,
           L0_ffn_bias):
    host = _host_arrays(
        np.asarray(L0_attn_data, dtype=np.float32),
        np.asarray(L0_attn_mask, dtype=np.int32),
        np.asarray(L0_attn_weight, dtype=np.float32),
        np.asarray(L0_attn_bias, dtype=np.float32),
        np.asarray(L0_ffn_weight_packed, dtype=np.int32),
        np.asarray(L0_ffn_weight_scales, dtype=np.float32),
        np.asarray(L0_ffn_weight_zeros, dtype=np.float32),
        np.asarray(L0_ffn_bias, dtype=np.float32),
    )
    nc = bass.Bass("TRN2", target_bir_lowering=False)
    build(nc)
    shared = {k: v for k, v in host.items() if k != "xT"}
    in_maps = []
    for c in range(N_CORES):
        m = dict(shared)
        m["xT"] = np.ascontiguousarray(host["xT"][:, c * S_CORE:(c + 1) * S_CORE])
        in_maps.append(m)
    res = run_bass_kernel_spmd(nc, in_maps, core_ids=list(range(N_CORES)))
    outT = np.concatenate([res.results[c]["outT"] for c in range(N_CORES)], axis=1)
    return np.ascontiguousarray(outT.T).reshape(2, 8192, D).astype(np.float32)
